# version 03
# baseline (speedup 1.0000x reference)
from contextlib import ExitStack

import numpy as np
import concourse.bass as bass
import concourse.mybir as mybir
from concourse.bass_utils import run_bass_kernel_spmd

F32 = mybir.dt.float32
FP16 = mybir.dt.float16
AF = mybir.ActivationFunctionType
ALU = mybir.AluOpType

DMODEL = 4096
DH = 2048
NDT = DH // 128
NST = 16
BNECK = 512
NEXP = 64
TC = 512
NTT = TC // 128
LN_EPS = 1e-5
RG = [[0, 1], [2, 3], [4, 5], [6, 7]]


def build_program(seq_len, debug=False):
    nch = seq_len // TC
    nc = bass.Bass("TRN2", target_bir_lowering=False, num_devices=8,
                   detect_race_conditions=False)

    def din(name, shape, dt=F32):
        return nc.dram_tensor(name, shape, dt, kind="ExternalInput").ap()

    x_in = din("x_in", [seq_len, DH])
    awt = din("awt", [DH, BNECK])
    bwt = din("bwt", [BNECK, DH])
    wqt = din("wqt", [DH, 16])
    keyst = din("keyst", [16, NEXP])
    xpt = din("xpt", [DH, 160])
    dtwt = din("dtwt", [128, DH])
    dtb = din("dtb", [128, NDT])
    alog = din("alog", [128, NDT * NST])
    dvec = din("dvec", [128, NDT])
    ident16 = din("ident16", [128, 128], FP16)
    ident32 = din("ident32", [128, 128], F32)
    sel = din("sel", [NEXP, 4 * 128], FP16)

    out = nc.dram_tensor("out", [seq_len, DH], F32, kind="ExternalOutput").ap()
    dbg = {}
    if debug:
        for nm, shape in [("dbg_x1", [seq_len, DH]), ("dbg_xh2", [seq_len, DH]),
                          ("dbg_yt", [128, NTT * DH]), ("dbg_x2", [seq_len, DH]),
                          ("dbg_L", [128, NDT * TC]), ("dbg_xh1", [TC, DH]),
                          ("dbg_st2", [128, 2 * NTT]), ("dbg_mv2", [128, 8 * NTT])]:
            dbg[nm] = nc.dram_tensor(nm, shape, F32, kind="ExternalOutput").ap()

    bnc = {}
    for ch in range(nch):
        for tag, shape in [("s1", [128, 2 * NTT]), ("s2", [128, 2 * NTT]),
                           ("s3", [128, 2 * NTT]), ("zq1", [132, 4 * TC]),
                           ("zq2", [132, 4 * TC]), ("xp", [160, TC])]:
            bnc[(ch, tag, "i")] = nc.dram_tensor(f"b_{tag}_{ch}_i", shape, F32)
            bnc[(ch, tag, "o")] = nc.dram_tensor(f"b_{tag}_{ch}_o", shape, F32)

    ctx = ExitStack()
    sb = lambda n, s, d: ctx.enter_context(nc.sbuf_tensor(n, s, d))
    psm = lambda n, s, d: ctx.enter_context(nc.psum_tensor(n, s, d))

    w_awt = sb("w_awt", [128, NDT * BNECK], FP16)
    w_bwt = sb("w_bwt", [128, 4 * DH], FP16)
    w_wqt = sb("w_wqt", [128, NDT * 16], FP16)
    w_keys = sb("w_keys", [16, NEXP], FP16)
    w_xpt = sb("w_xpt", [128, NDT * 160], FP16)
    w_dtwt = sb("w_dtwt", [128, DH], FP16)
    w_dtbn = sb("w_dtbn", [128, NDT], F32)
    w_absA = sb("w_absA", [128, NDT * NST], F32)
    w_D = sb("w_D", [128, NDT], F32)
    w_id16 = sb("w_id16", [128, 128], FP16)
    w_id32 = sb("w_id32", [128, 128], F32)
    w_sel = sb("w_sel", [64, 4 * 128], FP16)

    x1_sb = sb("x1_sb", [128, NTT * DH], F32)
    xh_a = sb("xh_a", [128, NTT * DH], FP16)
    xhT = sb("xhT", [128, 2 * TC], FP16)
    stats = sb("stats", [128, 2 * NTT], F32)
    stats_r = sb("stats_r", [128, 2 * NTT], F32)
    mvec = sb("mvec", [128, 8 * NTT], F32)
    zq_sb = sb("zq_sb", [128, 5 * TC], F32)
    zq_r = sb("zq_r", [128, 4 * TC], FP16)
    q16 = sb("q16", [16, TC], FP16)
    sc_sb = sb("sc_sb", [128, NEXP], F32)
    sc_mr = sb("sc_mr", [128, NEXP], F32)
    m8a = sb("m8a", [128, 8], F32)
    m8b = sb("m8b", [128, 8], F32)
    gex = sb("gex", [128, NTT * NEXP], F32)
    gmask = sb("gmask", [128, NEXP], F32)
    gate = sb("gate", [128, NTT * NEXP], F32)
    gateT = sb("gateT", [64, TC], FP16)
    grep = sb("grep", [128, 4 * TC], FP16)
    zgT = sb("zgT", [128, 4 * TC], FP16)
    xp_d = sb("xp_d", [128, TC], F32)
    xp_bc = sb("xp_bc", [32, TC], F32)
    dtin = sb("dtin", [128, TC], FP16)
    b_rep = sb("b_rep", [128, NST * TC], FP16)
    c_rep = sb("c_rep", [128, NST * TC], FP16)
    tsg = sb("tsg", [128, TC], F32)
    L_d = sb("L_d", [128, 2 * TC], FP16)
    u_d = sb("u_d", [128, 2 * TC], FP16)
    x2T_d = sb("x2T_d", [128, 2 * TC], FP16)
    dA_b = sb("dA_b", [128, 2 * TC], FP16)
    dBx_b = sb("dBx_b", [128, 2 * TC], FP16)
    h_b = sb("h_b", [128, 2 * TC], FP16)
    hC_b = sb("hC_b", [128, TC], FP16)
    accs = sb("accs", [128, 4 * TC], FP16)
    carry = sb("carry", [128, NDT * NST], FP16)
    ytot = sb("ytot", [128, TC], F32)
    yT_sb = sb("yT_sb", [128, NTT * DH], FP16)

    ps_z = psm("ps_z", [128, 2048], F32)
    ps_a = psm("ps_a", [128, 512], F32)
    ps_b = psm("ps_b", [128, 512], F32)
    ps_tr = psm("ps_tr", [128, 1024], FP16)
    ps_t32 = psm("ps_t32", [128, 512], F32)

    sems = {n: ctx.enter_context(nc.semaphore(n))
            for n in ["sd", "gd", "cc", "sv", "ss", "st", "sg"]}
    cnt = {n: 0 for n in sems}

    blk = ctx.enter_context(nc.Block())

    def on(eng, fn):
        getattr(blk, eng)(fn)

    def bump(name, k=1):
        cnt[name] += k
        return cnt[name]

    def dma(q, out_ap, in_ap, s="sd"):
        q.dma_start(out=out_ap, in_=in_ap).then_inc(sems[s], 16)
        return bump(s, 16)

    def barrier():
        targets = {"sv": cnt["sv"], "ss": cnt["ss"], "st": cnt["st"],
                   "gd": cnt["gd"], "sd": cnt["sd"], "cc": cnt["cc"]}
        marks = {}
        for eng, sem in [("vector", "sv"), ("scalar", "ss"), ("tensor", "st")]:
            def _(e, sem=sem):
                for s2, v2 in targets.items():
                    if v2:
                        e.wait_ge(sems[s2], v2)
                e.sem_inc(sems[sem], 1)
            on(eng, _)
            marks[sem] = bump(sem)
        for eng, sem in [("vector", "sv"), ("scalar", "ss"), ("tensor", "st")]:
            def _(e, sem=sem):
                for s2 in ("sv", "ss", "st"):
                    e.wait_ge(sems[s2], marks[s2])
            on(eng, _)
        def _(g):
            for s2, v2 in targets.items():
                if v2:
                    g.wait_ge(sems[s2], v2)
            for s2 in ("sv", "ss", "st"):
                g.wait_ge(sems[s2], marks[s2])
        on("gpsimd", _)
        return marks

    def _(g):
        for kd in range(NDT):
            dma(g, w_awt[:, kd * BNECK:(kd + 1) * BNECK],
                awt[kd * 128:(kd + 1) * 128, :], "gd")
        for kb in range(4):
            dma(g, w_bwt[:, kb * DH:(kb + 1) * DH],
                bwt[kb * 128:(kb + 1) * 128, :], "gd")
        for kd in range(NDT):
            dma(g, w_wqt[:, kd * 16:(kd + 1) * 16],
                wqt[kd * 128:(kd + 1) * 128, :], "gd")
        dma(g, w_keys[:, :], keyst, "gd")
        for kd in range(NDT):
            dma(g, w_xpt[:, kd * 160:(kd + 1) * 160],
                xpt[kd * 128:(kd + 1) * 128, :], "gd")
        dma(g, w_dtwt[:, :], dtwt, "gd")
        dma(g, w_id16[:, :], ident16, "gd")
        dma(g, w_id32[:, :], ident32, "gd")
        dma(g, w_sel[:, :], sel, "gd")
        dma(g, w_dtbn[:, :], dtb, "gd")
        dma(g, w_absA[:, :], alog, "gd")
        dma(g, w_D[:, :], dvec, "gd")
    on("gpsimd", _)
    wmark = cnt["gd"]

    def _(v):
        v.wait_ge(sems["gd"], wmark)
        v.memset(carry[:, :], 0.0).then_inc(sems["sv"], 1)
    on("vector", _)
    prol_v = bump("sv")

    def _(s):
        s.wait_ge(sems["gd"], wmark)
        s.activation(out=w_absA[:, :], in_=w_absA[:, :],
                     func=AF.Exp).then_inc(sems["ss"], 1)
        s.wait_ge(sems["ss"], bump("ss"))
        s.mul(w_absA[:, :], w_absA[:, :], -1.0).then_inc(sems["ss"], 1)
    on("scalar", _)
    prol_s = bump("ss")

    last_xhT_reader = [0, 0]
    last_pstr_evac = [0, 0]

    def ln_stats_norm(ch, tag, dst, dep_sv):
        mk = {}

        def _(v):
            v.wait_ge(sems["sv"], dep_sv)
            for tt in range(NTT):
                ins = v.tensor_reduce(out=stats[:, 2 * tt:2 * tt + 1],
                                      in_=x1_sb[:, tt * DH:(tt + 1) * DH],
                                      axis=mybir.AxisListType.X, op=ALU.add)
            ins.then_inc(sems["sv"], 1)
            mk["v"] = bump("sv")
        on("vector", _)

        def _(s):
            s.wait_ge(sems["sv"], dep_sv)
            for tt in range(NTT):
                ins = s.activation(out=xh_a[:, tt * DH:(tt + 1) * DH],
                                   in_=x1_sb[:, tt * DH:(tt + 1) * DH],
                                   func=AF.Square)
            ins.then_inc(sems["ss"], 1)
            mk["s"] = bump("ss")
        on("scalar", _)

        def _(v):
            v.wait_ge(sems["ss"], mk["s"])
            for tt in range(NTT):
                ins = v.tensor_reduce(out=stats[:, 2 * tt + 1:2 * tt + 2],
                                      in_=xh_a[:, tt * DH:(tt + 1) * DH],
                                      axis=mybir.AxisListType.X, op=ALU.add)
            ins.then_inc(sems["sv"], 1)
            mk["s"] = bump("sv")
            mk["s_is_sv"] = True
        on("vector", _)

        def _(g):
            g.wait_ge(sems["sv"], mk["v"])
            g.wait_ge(sems["sv"], mk["s"])
            dma(g, bnc[(ch, tag, "i")].ap(), stats[:, :], "gd")
            g.wait_ge(sems["gd"], cnt["gd"])
            g.collective_compute(
                "AllReduce", ALU.add, replica_groups=RG,
                ins=[bnc[(ch, tag, "i")].ap()], outs=[bnc[(ch, tag, "o")].ap()],
            ).then_inc(sems["cc"], 1)
            bump("cc")
            g.wait_ge(sems["cc"], cnt["cc"])
            dma(g, stats_r[:, :], bnc[(ch, tag, "o")].ap(), "gd")
            mk["gd"] = cnt["gd"]
        on("gpsimd", _)

        def _(v):
            v.wait_ge(sems["gd"], mk["gd"])
            mm = mvec[:, :].rearrange("p (t k) -> p t k", k=8)
            m_ = mm[:, :, 0:1]
            v_ = mm[:, :, 1:2]
            sq_ = mm[:, :, 4:5]
            sr = stats_r[:, :].rearrange("p (t k) -> p t k", k=2)
            v.tensor_scalar_mul(m_, sr[:, :, 0:1], 1.0 / DMODEL)
            v.tensor_scalar_mul(v_, sr[:, :, 1:2], 1.0 / DMODEL).then_inc(
                sems["sv"], 1)
            v.wait_ge(sems["sv"], bump("sv"))
            v.tensor_mul(sq_, m_, m_).then_inc(sems["sv"], 1)
            v.wait_ge(sems["sv"], bump("sv"))
            v.tensor_sub(v_, v_, sq_).then_inc(sems["sv"], 1)
            v.wait_ge(sems["sv"], bump("sv"))
            v.tensor_scalar_add(v_, v_, LN_EPS).then_inc(sems["sv"], 1)
            mk["v2"] = bump("sv")
        on("vector", _)

        def _(s):
            s.wait_ge(sems["sv"], mk["v2"])
            mm = mvec[:, :].rearrange("p (t k) -> p t k", k=8)
            s.activation(out=mm[:, :, 6:7], in_=mm[:, :, 1:2],
                         func=AF.Ln).then_inc(sems["ss"], 1)
            s.wait_ge(sems["ss"], bump("ss"))
            s.activation(out=mm[:, :, 5:6], in_=mm[:, :, 6:7],
                         func=AF.Exp, scale=0.5).then_inc(sems["ss"], 1)
            mk["s2"] = bump("ss")
        on("scalar", _)

        def _(v):
            v.wait_ge(sems["ss"], mk["s2"])
            mm = mvec[:, :].rearrange("p (t k) -> p t k", k=8)
            m_ = mm[:, :, 0:1]
            v_ = mm[:, :, 1:2]
            i_ = mm[:, :, 2:3]
            nmi = mm[:, :, 3:4]
            t_ = mm[:, :, 4:5]
            sd_ = mm[:, :, 5:6]
            def sw(ins):
                ins.then_inc(sems["sv"], 1)
                v.wait_ge(sems["sv"], bump("sv"))
            sw(v.reciprocal(i_, sd_))
            sw(v.tensor_mul(t_, i_, i_))
            sw(v.tensor_mul(t_, t_, v_))
            sw(v.tensor_scalar(out=t_, in0=t_, scalar1=-0.5, scalar2=1.5,
                               op0=ALU.mult, op1=ALU.add))
            sw(v.tensor_mul(i_, i_, t_))
            sw(v.tensor_mul(nmi, m_, i_))
            v.tensor_scalar_mul(nmi, nmi, -1.0).then_inc(sems["sv"], 1)
            mk["v3"] = bump("sv")
        on("vector", _)

        def _(v):
            v.wait_ge(sems["sv"], mk["v3"])
            for tt in range(NTT):
                ins = v.tensor_scalar(out=dst[:, tt * DH:(tt + 1) * DH],
                                      in0=x1_sb[:, tt * DH:(tt + 1) * DH],
                                      scalar1=mvec[:, 8 * tt + 2:8 * tt + 3],
                                      scalar2=mvec[:, 8 * tt + 3:8 * tt + 4],
                                      op0=ALU.mult, op1=ALU.add)
            ins.then_inc(sems["sv"], 1)
            mk["nm"] = bump("sv")
        on("vector", _)
        def _(s):
            s.wait_ge(sems["sv"], mk["nm"])
            s.sem_inc(sems["ss"], 1)
        on("scalar", _)
        return bump("ss")

    def dmajor_sweep(src, norm_mark, mm_emit):
        tmarks = {}
        emarks = {}
        mmarks = {}
        for kd in range(NDT):
            half = kd % 2

            def _(te, kd=kd, half=half):
                if kd == 0:
                    te.wait_ge(sems["gd"], wmark)
                    te.wait_ge(sems["ss"], norm_mark)
                te.wait_ge(sems["ss"], last_pstr_evac[half])
                for tt in range(NTT):
                    ins = te.transpose(out=ps_tr[:, half * 512 + tt * 128:
                                               half * 512 + (tt + 1) * 128],
                                       in_=src[:, tt * DH + kd * 128:
                                               tt * DH + (kd + 1) * 128],
                                       identity=w_id16[:, :])
                ins.then_inc(sems["st"], 1)
                tmarks[kd] = bump("st")
            on("tensor", _)

            def _(s, kd=kd, half=half):
                s.wait_ge(sems["st"], tmarks[kd])
                s.wait_ge(sems["st"], last_xhT_reader[half])
                s.copy(out=xhT[:, half * TC:(half + 1) * TC],
                       in_=ps_tr[:, half * 512:(half + 1) * 512]).then_inc(
                           sems["ss"], 1)
                emarks[kd] = bump("ss")
                last_pstr_evac[half] = emarks[kd]
            on("scalar", _)

            def _(te, kd=kd, half=half):
                te.wait_ge(sems["ss"], emarks[kd])
                ins = mm_emit(te, kd, half)
                ins.then_inc(sems["st"], 1)
                mmarks[kd] = bump("st")
                last_xhT_reader[half] = mmarks[kd]
            on("tensor", _)
        return mmarks[NDT - 1]

    def emit_ffn(ch, norm_mark, tag):
        mk = {}

        def zmm(te, kd, half):
            for mm in range(4):
                te.matmul(out=ps_z[:, mm * 512:(mm + 1) * 512],
                          lhsT=w_awt[:, kd * BNECK + mm * 128:
                                     kd * BNECK + (mm + 1) * 128],
                          rhs=xhT[:, half * TC:(half + 1) * TC],
                          start=(kd == 0), stop=(kd == NDT - 1))
            return te.matmul(out=ps_a[0:16, :],
                             lhsT=w_wqt[:, kd * 16:(kd + 1) * 16],
                             rhs=xhT[:, half * TC:(half + 1) * TC],
                             start=(kd == 0), stop=(kd == NDT - 1))
        mk["mm"] = dmajor_sweep(xh_a, norm_mark, zmm)

        def _(s):
            s.wait_ge(sems["st"], mk["mm"])
            for mm in range(4):
                s.copy(out=zq_sb[:, mm * TC:(mm + 1) * TC],
                       in_=ps_z[:, mm * 512:(mm + 1) * 512])
            s.copy(out=zq_sb[0:16, 4 * TC:5 * TC],
                   in_=ps_a[0:16, :]).then_inc(sems["ss"], 1)
            mk["ev"] = bump("ss")
        on("scalar", _)

        def _(g):
            g.wait_ge(sems["ss"], mk["ev"])
            dma(g, bnc[(ch, tag, "i")].ap()[0:128, :], zq_sb[:, 0:4 * TC], "gd")
            dma(g, bnc[(ch, tag, "i")].ap()[128:132, :].flatten()
                .rearrange("(p t) -> p t", p=16),
                zq_sb[0:16, 4 * TC:5 * TC], "gd")
            g.wait_ge(sems["gd"], cnt["gd"])
            g.collective_compute(
                "AllReduce", ALU.add, replica_groups=RG,
                ins=[bnc[(ch, tag, "i")].ap()], outs=[bnc[(ch, tag, "o")].ap()],
            ).then_inc(sems["cc"], 1)
            bump("cc")
            g.wait_ge(sems["cc"], cnt["cc"])
            dma(g, zq_r[:, :], bnc[(ch, tag, "o")].ap()[0:128, :], "gd")
            dma(g, q16[:, :], bnc[(ch, tag, "o")].ap()[128:132, :].flatten()
                .rearrange("(p t) -> p t", p=16), "gd")
            mk["gd"] = cnt["gd"]
        on("gpsimd", _)

        def _(te):
            te.wait_ge(sems["gd"], mk["gd"])
            for tt in range(NTT):
                ins = te.matmul(out=ps_b[:, tt * 64:(tt + 1) * 64],
                                lhsT=q16[:, tt * 128:(tt + 1) * 128],
                                rhs=w_keys[:, :], start=True, stop=True)
            ins.then_inc(sems["st"], 1)
            mk["sc"] = bump("st")
        on("tensor", _)

        def _(v):
            v.wait_ge(sems["st"], mk["sc"])
            for tt in range(NTT):
                v.tensor_copy(sc_sb[:, :], ps_b[:, tt * 64:(tt + 1) * 64])
                v.max(out=m8a[:, :], in_=sc_sb[:, :]).then_inc(sems["sv"], 1)
                v.wait_ge(sems["sv"], bump("sv"))
                v.match_replace(out=sc_mr[:, :], in_to_replace=m8a[:, :],
                                in_values=sc_sb[:, :],
                                imm_value=-3e38).then_inc(sems["sv"], 1)
                v.wait_ge(sems["sv"], bump("sv"))
                v.max(out=m8b[:, :], in_=sc_mr[:, :]).then_inc(sems["sv"], 1)
                v.wait_ge(sems["sv"], bump("sv"))
                v.tensor_scalar_mul(mvec[:, 8 * tt + 6:8 * tt + 7],
                                    m8a[:, 0:1], -1.0)
                v.tensor_copy(mvec[:, 8 * tt + 5:8 * tt + 6],
                              m8b[:, 7:8]).then_inc(sems["sv"], 1)
                mk[f"g1_{tt}"] = bump("sv")
        on("vector", _)

        def _(s):
            for tt in range(NTT):
                s.wait_ge(sems["sv"], mk[f"g1_{tt}"])
                s.activation(out=gex[:, tt * NEXP:(tt + 1) * NEXP],
                             in_=ps_b[:, tt * 64:(tt + 1) * 64], func=AF.Exp,
                             bias=mvec[:, 8 * tt + 6:8 * tt + 7]).then_inc(
                                 sems["ss"], 1)
                mk[f"e{tt}"] = bump("ss")
        on("scalar", _)

        def _(v):
            for tt in range(NTT):
                gx = gex[:, tt * NEXP:(tt + 1) * NEXP]
                v.wait_ge(sems["ss"], mk[f"e{tt}"])
                v.tensor_single_scalar(gmask[:, :], ps_b[:, tt * 64:(tt + 1) * 64],
                                       mvec[:, 8 * tt + 5:8 * tt + 6], ALU.is_ge)
                v.tensor_mul(gx, gx, gmask[:, :])
                v.tensor_reduce(out=mvec[:, 8 * tt + 7:8 * tt + 8],
                                in_=gx, axis=mybir.AxisListType.X,
                                op=ALU.add).then_inc(sems["sv"], 1)
                v.wait_ge(sems["sv"], bump("sv"))
                v.reciprocal(mvec[:, 8 * tt + 6:8 * tt + 7],
                             mvec[:, 8 * tt + 7:8 * tt + 8]).then_inc(
                                 sems["sv"], 1)
                v.wait_ge(sems["sv"], bump("sv"))
                ins = v.tensor_scalar_mul(gate[:, tt * NEXP:(tt + 1) * NEXP],
                                          gx, mvec[:, 8 * tt + 6:8 * tt + 7])
            ins.then_inc(sems["sv"], 1)
            mk["g2"] = bump("sv")
        on("vector", _)

        def _(te):
            te.wait_ge(sems["sv"], mk["g2"])
            for tt in range(NTT):
                ins = te.transpose(out=ps_t32[0:64, tt * 128:(tt + 1) * 128],
                                   in_=gate[:, tt * NEXP:(tt + 1) * NEXP],
                                   identity=w_id32[:, :])
            ins.then_inc(sems["st"], 1)
            mk["gt"] = bump("st")
        on("tensor", _)

        def _(s):
            s.wait_ge(sems["st"], mk["gt"])
            s.copy(out=gateT[:, :],
                   in_=ps_t32[0:64, 0:TC]).then_inc(sems["ss"], 1)
            mk["gte"] = bump("ss")
        on("scalar", _)

        gr_marks = {}
        for mm in range(4):
            def _(te, mm=mm):
                te.wait_ge(sems["ss"], mk["gte"] if mm == 0 else gr_marks[(mm - 1, "e")])
                te.matmul(out=ps_a[:, :], lhsT=w_sel[:, mm * 128:(mm + 1) * 128],
                          rhs=gateT[:, :], start=True,
                          stop=True).then_inc(sems["st"], 1)
                gr_marks[(mm, "t")] = bump("st")
            on("tensor", _)

            def _(s, mm=mm):
                s.wait_ge(sems["st"], gr_marks[(mm, "t")])
                s.copy(out=grep[:, mm * TC:(mm + 1) * TC],
                       in_=ps_a[:, :]).then_inc(sems["ss"], 1)
                gr_marks[(mm, "e")] = bump("ss")
            on("scalar", _)
        mk["gre"] = gr_marks[(3, "e")]

        def _(v):
            v.wait_ge(sems["ss"], mk["gre"])
            v.wait_ge(sems["gd"], mk["gd"])
            for mm in range(4):
                ins = v.tensor_mul(zgT[:, mm * TC:(mm + 1) * TC],
                                   zq_r[:, mm * TC:(mm + 1) * TC],
                                   grep[:, mm * TC:(mm + 1) * TC])
            ins.then_inc(sems["sv"], 1)
            mk["zg"] = bump("sv")
        on("vector", _)
        return mk

    def ffn_out2(mk, store_cb):
        prev_store = None
        last = {}
        for tt in range(NTT):
            def _(te, tt=tt, prev_store=prev_store):
                if tt == 0:
                    te.wait_ge(sems["sv"], mk["zg"])
                else:
                    te.wait_ge(sems["sv"], prev_store)
                for kb in range(4):
                    for q4 in range(4):
                        ins = te.matmul(out=ps_z[:, q4 * 512:(q4 + 1) * 512],
                                        lhsT=zgT[:, kb * TC + tt * 128:
                                                 kb * TC + (tt + 1) * 128],
                                        rhs=w_bwt[:, kb * DH + q4 * 512:
                                                  kb * DH + (q4 + 1) * 512],
                                        start=(kb == 0), stop=(kb == 3))
                ins.then_inc(sems["st"], 1)
                last["mm"] = bump("st")
            on("tensor", _)
            prev_store = store_cb(tt, last["mm"])
        return prev_store

    prev_chunk_done = prol_v
    prev_out_sd = {"m": 0}
    for ch in range(nch):
        t0 = ch * TC

        ldm = {}

        def _(sync, t0=t0, dep=prev_chunk_done, odep=prev_out_sd["m"]):
            sync.wait_ge(sems["sv"], dep)
            if odep:
                sync.wait_ge(sems["sd"], odep)
            for tt in range(NTT):
                dma(sync, x1_sb[:, tt * DH:(tt + 1) * DH],
                    x_in[t0 + tt * 128:t0 + (tt + 1) * 128, :], "sd")
            ldm["x"] = cnt["sd"]
        on("sync", _)

        def _(v):
            v.wait_ge(sems["sd"], ldm["x"])
            v.wait_ge(sems["ss"], prol_s)
            v.sem_inc(sems["sv"], 1)
        on("vector", _)
        x_ready = bump("sv")

        nm1 = ln_stats_norm(ch, "s1", xh_a, x_ready)
        mkf1 = emit_ffn(ch, nm1, "zq1")

        def store_x1(tt, mm_mark):
            def _(v, tt=tt, mm_mark=mm_mark):
                v.wait_ge(sems["st"], mm_mark)
                v.tensor_add(x1_sb[:, tt * DH:(tt + 1) * DH], ps_z[:, :],
                             x1_sb[:, tt * DH:(tt + 1) * DH]).then_inc(
                                 sems["sv"], 1)
            on("vector", _)
            return bump("sv")
        x1_mark = ffn_out2(mkf1, store_x1)

        if debug:
            def _(g, m=x1_mark, t0=t0):
                g.wait_ge(sems["sv"], m)
                for tt in range(NTT):
                    dma(g, dbg["dbg_x1"][t0 + tt * 128:t0 + (tt + 1) * 128, :],
                        x1_sb[:, tt * DH:(tt + 1) * DH], "gd")
            on("gpsimd", _)

        if debug:
            def _(g, nm=nm1, ch=ch):
                if ch == 0:
                    g.wait_ge(sems["ss"], nm)
                    for tt in range(NTT):
                        dma(g, dbg["dbg_xh1"][tt * 128:(tt + 1) * 128, :],
                            xh_a[:, tt * DH:(tt + 1) * DH], "gd")
            on("gpsimd", _)

        nm2 = ln_stats_norm(ch, "s2", xh_a, x1_mark)
        if debug:
            def _(g, nm=nm2, ch=ch):
                if ch == 0:
                    g.wait_ge(sems["ss"], nm)
                    dma(g, dbg["dbg_st2"], stats_r[:, :], "gd")
                    dma(g, dbg["dbg_mv2"], mvec[:, :], "gd")
            on("gpsimd", _)

        if debug:
            def _(g, nm=nm2, t0=t0):
                g.wait_ge(sems["ss"], nm)
                for tt in range(NTT):
                    dma(g, dbg["dbg_xh2"][t0 + tt * 128:t0 + (tt + 1) * 128, :],
                        xh_a[:, tt * DH:(tt + 1) * DH], "gd")
            on("gpsimd", _)

        mkxp = {}

        def xpmm(te, kd, half):
            te.matmul(out=ps_a[:, :],
                      lhsT=w_xpt[:, kd * 160:kd * 160 + 128],
                      rhs=xhT[:, half * TC:(half + 1) * TC],
                      start=(kd == 0), stop=(kd == NDT - 1))
            return te.matmul(out=ps_b[0:32, :],
                             lhsT=w_xpt[:, kd * 160 + 128:(kd + 1) * 160],
                             rhs=xhT[:, half * TC:(half + 1) * TC],
                             start=(kd == 0), stop=(kd == NDT - 1))
        mkxp["mm"] = dmajor_sweep(xh_a, nm2, xpmm)

        def _(s):
            s.wait_ge(sems["st"], mkxp["mm"])
            s.copy(out=xp_d[:, :], in_=ps_a[:, :])
            s.copy(out=xp_bc[:, :], in_=ps_b[0:32, :]).then_inc(sems["ss"], 1)
            mkxp["ev"] = bump("ss")
        on("scalar", _)

        def _(g, ch=ch):
            g.wait_ge(sems["ss"], mkxp["ev"])
            dma(g, bnc[(ch, "xp", "i")].ap()[0:128, :], xp_d[:, :], "gd")
            dma(g, bnc[(ch, "xp", "i")].ap()[128:160, :], xp_bc[:, :], "gd")
            g.wait_ge(sems["gd"], cnt["gd"])
            g.collective_compute(
                "AllReduce", ALU.add, replica_groups=RG,
                ins=[bnc[(ch, "xp", "i")].ap()], outs=[bnc[(ch, "xp", "o")].ap()],
            ).then_inc(sems["cc"], 1)
            bump("cc")
            g.wait_ge(sems["cc"], cnt["cc"])
            dma(g, dtin[:, :], bnc[(ch, "xp", "o")].ap()[0:128, :], "gd")
            mkxp["gd"] = cnt["gd"]
            xp_o = bnc[(ch, "xp", "o")].ap()
            dma(g, b_rep[0:1, :],
                xp_o[128:144, :].flatten().unsqueeze(0), "gd")
            dma(g, c_rep[0:1, :],
                xp_o[144:160, :].flatten().unsqueeze(0), "gd")
            k = 1
            while k < 128:
                kk = min(k, 128 - k)
                g.wait_ge(sems["gd"], cnt["gd"])
                dma(g, b_rep[k:k + kk, :], b_rep[0:kk, :], "gd")
                dma(g, c_rep[k:k + kk, :], c_rep[0:kk, :], "gd")
                k += kk
            g.wait_ge(sems["gd"], cnt["gd"])
            g.sem_inc(sems["sg"], 1)
            mkxp["rep"] = bump("sg")
        on("gpsimd", _)

        barrier()
        prev_mkd = [None, None]
        last_dt = None
        prev_scan = {"last": None}
        for dt_i in range(NDT):
            mkd = {}
            pL = dt_i % 2
            pOld = prev_mkd[pL]
            pAny = last_dt

            def _(te, dt_i=dt_i, mkd=mkd, pAny=pAny):
                te.wait_ge(sems["gd"], mkxp["gd"])
                if pAny is not None:
                    te.wait_ge(sems["ss"], pAny["L"])
                te.matmul(out=ps_a[:, :],
                          lhsT=w_dtwt[:, dt_i * 128:(dt_i + 1) * 128],
                          rhs=dtin[:, :], start=True,
                          stop=True).then_inc(sems["st"], 1)
                mkd["dt"] = bump("st")
            on("tensor", _)

            def _(s, dt_i=dt_i, mkd=mkd, pL=pL, pOld=pOld):
                s.wait_ge(sems["st"], mkd["dt"])
                if pOld is not None:
                    s.wait_ge(sems["sv"], pOld["y"])
                s.activation(out=tsg[:, :], in_=ps_a[:, :], func=AF.Exp,
                             bias=w_dtbn[:, dt_i:dt_i + 1]).then_inc(
                                 sems["ss"], 1)
                s.wait_ge(sems["ss"], bump("ss"))
                s.activation(out=L_d[:, pL * TC:(pL + 1) * TC], in_=tsg[:, :],
                             func=AF.Ln, bias=1.0).then_inc(sems["ss"], 1)
                mkd["L"] = bump("ss")
            on("scalar", _)

            if debug:
                def _(g, dt_i=dt_i, mkd=mkd, pL=pL, ch=ch):
                    if ch == 0:
                        g.wait_ge(sems["ss"], mkd["L"])
                        dma(g, dbg["dbg_L"][:, dt_i * TC:(dt_i + 1) * TC],
                            L_d[:, pL * TC:(pL + 1) * TC], "gd")
                on("gpsimd", _)

            def _(te, dt_i=dt_i, mkd=mkd, pL=pL):
                te.wait_ge(sems["ss"], nm2)
                te.wait_ge(sems["ss"], last_pstr_evac[pL])
                for tt in range(NTT):
                    ins = te.transpose(out=ps_tr[:, pL * 512 + tt * 128:
                                               pL * 512 + (tt + 1) * 128],
                                       in_=xh_a[:, tt * DH + dt_i * 128:
                                                tt * DH + (dt_i + 1) * 128],
                                       identity=w_id16[:, :])
                ins.then_inc(sems["st"], 1)
                mkd["x2tt"] = bump("st")
            on("tensor", _)

            def _(s, mkd=mkd, pL=pL, pOld=pOld):
                s.wait_ge(sems["st"], mkd["x2tt"])
                if pOld is not None:
                    s.wait_ge(sems["sv"], pOld["y"])
                s.copy(out=x2T_d[:, pL * TC:(pL + 1) * TC],
                       in_=ps_tr[:, pL * 512:(pL + 1) * 512]).then_inc(
                           sems["ss"], 1)
                mkd["x2te"] = bump("ss")
                last_pstr_evac[pL] = mkd["x2te"]
            on("scalar", _)

            scan_base = {}
            scan_base["b"] = cnt["sv"] + 1
            prev_last_scan = prev_scan["last"]

            def _(s, dt_i=dt_i, mkd=mkd, pL=pL, pls=prev_last_scan):
                for n in range(NST):
                    if n >= 2:
                        s.wait_ge(sems["sv"], scan_base["b"] + (n - 1))
                    elif pls is not None:
                        s.wait_ge(sems["sv"], pls - 1 + n)
                    s.activation(out=dA_b[:, (n % 2) * TC:(n % 2 + 1) * TC],
                                 in_=L_d[:, pL * TC:(pL + 1) * TC], func=AF.Exp,
                                 scale=w_absA[:, dt_i * NST + n:
                                              dt_i * NST + n + 1]).then_inc(
                                     sems["ss"], 1)
                    mkd[f"dA{n}"] = bump("ss")
            on("scalar", _)

            def _(v, dt_i=dt_i, mkd=mkd, pL=pL, ch=ch, pOld=pOld, pAny=pAny):
                v.wait_ge(sems["ss"], mkd["L"])
                v.wait_ge(sems["ss"], mkd["x2te"])
                v.wait_ge(sems["sg"], mkxp["rep"])
                if pAny is not None:
                    v.wait_ge(sems["st"], pAny["yt"])
                v.tensor_mul(u_d[:, pL * TC:(pL + 1) * TC],
                             L_d[:, pL * TC:(pL + 1) * TC],
                             x2T_d[:, pL * TC:(pL + 1) * TC]).then_inc(
                                 sems["sv"], 1)
                assert bump("sv") == scan_base["b"]
                for n in range(NST):
                    pb = n % 2
                    v.tensor_mul(dBx_b[:, pb * TC:(pb + 1) * TC],
                                 u_d[:, pL * TC:(pL + 1) * TC],
                                 b_rep[:, n * TC:(n + 1) * TC])
                    v.wait_ge(sems["ss"], mkd[f"dA{n}"])
                    idx = dt_i * NST + n
                    ins = v.tensor_tensor_scan(
                        out=h_b[:, pb * TC:(pb + 1) * TC],
                        data0=dA_b[:, pb * TC:(pb + 1) * TC],
                        data1=dBx_b[:, pb * TC:(pb + 1) * TC],
                        initial=carry[:, idx:idx + 1],
                        op0=ALU.mult, op1=ALU.add)
                    ins.then_inc(sems["sv"], 1)
                    bump("sv")
                    v.tensor_mul(hC_b[:, :], h_b[:, pb * TC:(pb + 1) * TC],
                                 c_rep[:, n * TC:(n + 1) * TC])
                    v.tensor_copy(carry[:, idx:idx + 1],
                                  h_b[:, pb * TC + TC - 1:pb * TC + TC])
                    j = n % 4
                    if n < 4:
                        v.tensor_copy(accs[:, j * TC:(j + 1) * TC], hC_b[:, :])
                    else:
                        v.tensor_add(accs[:, j * TC:(j + 1) * TC],
                                     accs[:, j * TC:(j + 1) * TC], hC_b[:, :])
                v.tensor_add(accs[:, 0:TC], accs[:, 0:TC], accs[:, TC:2 * TC])
                v.tensor_add(accs[:, 2 * TC:3 * TC], accs[:, 2 * TC:3 * TC],
                             accs[:, 3 * TC:4 * TC])
                v.tensor_add(accs[:, 0:TC], accs[:, 0:TC], accs[:, 2 * TC:3 * TC])
                v.scalar_tensor_tensor(out=ytot[:, :],
                                       in0=x2T_d[:, pL * TC:(pL + 1) * TC],
                                       scalar=w_D[:, dt_i:dt_i + 1],
                                       in1=accs[:, 0:TC],
                                       op0=ALU.mult,
                                       op1=ALU.add).then_inc(
                                           sems["sv"], 1)
                mkd["y"] = bump("sv")
            on("vector", _)
            prev_scan["last"] = scan_base["b"] + NST

            def _(te, mkd=mkd, pL=pL, pAny=pAny):
                te.wait_ge(sems["sv"], mkd["y"])
                if pAny is not None:
                    te.wait_ge(sems["ss"], pAny["yte"])
                for tt in range(NTT):
                    ins = te.transpose(out=ps_t32[:, tt * 128:(tt + 1) * 128],
                                       in_=ytot[:, tt * 128:(tt + 1) * 128],
                                       identity=w_id32[:, :])
                ins.then_inc(sems["st"], 1)
                mkd["yt"] = bump("st")
            on("tensor", _)

            def _(s, dt_i=dt_i, mkd=mkd):
                s.wait_ge(sems["st"], mkd["yt"])
                s.copy(out=yT_sb[:, :].rearrange("p (tt c) -> p tt c", tt=NTT)
                       [:, :, dt_i * 128:(dt_i + 1) * 128],
                       in_=ps_t32[:, :].rearrange(
                           "p (tt c) -> p tt c", tt=NTT)).then_inc(
                               sems["ss"], 1)
                mkd["yte"] = bump("ss")
            on("scalar", _)
            prev_mkd[pL] = mkd
            last_dt = mkd

        y_last = [prev_mkd[0]["yte"], prev_mkd[1]["yte"]]

        if debug:
            def _(g, y_last=y_last, ch=ch):
                if ch == 0:
                    g.wait_ge(sems["ss"], max(y_last))
                    dma(g, dbg["dbg_yt"], yT_sb[:, :], "gd")
            on("gpsimd", _)

        def _(v, y_last=y_last):
            v.wait_ge(sems["ss"], max(y_last))
            for tt in range(NTT):
                ins = v.tensor_add(x1_sb[:, tt * DH:(tt + 1) * DH],
                                   x1_sb[:, tt * DH:(tt + 1) * DH],
                                   yT_sb[:, tt * DH:(tt + 1) * DH])
            ins.then_inc(sems["sv"], 1)
        on("vector", _)
        x2_mark = bump("sv")

        if debug:
            def _(g, m=x2_mark, t0=t0):
                g.wait_ge(sems["sv"], m)
                for tt in range(NTT):
                    dma(g, dbg["dbg_x2"][t0 + tt * 128:t0 + (tt + 1) * 128, :],
                        x1_sb[:, tt * DH:(tt + 1) * DH], "gd")
            on("gpsimd", _)

        nm3 = ln_stats_norm(ch, "s3", xh_a, x2_mark)
        mkf2 = emit_ffn(ch, nm3, "zq2")
        out_sd = {"m": 0}

        def store_out(tt, mm_mark, t0=t0):
            def _(v, tt=tt, mm_mark=mm_mark):
                v.wait_ge(sems["st"], mm_mark)
                v.tensor_add(x1_sb[:, tt * DH:(tt + 1) * DH],
                             ps_z[:, :],
                             x1_sb[:, tt * DH:(tt + 1) * DH]).then_inc(
                                 sems["sv"], 1)
            on("vector", _)
            m = bump("sv")

            def _(sync, tt=tt, m=m, t0=t0):
                sync.wait_ge(sems["sv"], m)
                dma(sync, out[t0 + tt * 128:t0 + (tt + 1) * 128, :],
                    x1_sb[:, tt * DH:(tt + 1) * DH], "sd")
                out_sd["m"] = cnt["sd"]
            on("sync", _)
            return m
        last_store = ffn_out2(mkf2, store_out)
        prev_chunk_done = last_store
        prev_out_sd["m"] = out_sd["m"]

    def _(sync):
        sync.wait_ge(sems["sd"], cnt["sd"])
    on("sync", _)

    def _(g):
        g.wait_ge(sems["sd"], cnt["sd"])
        g.wait_ge(sems["gd"], cnt["gd"])
    on("gpsimd", _)

    ctx.close()
    return nc


def _shard_inputs(inputs):
    xs = np.asarray(inputs["L0_attn_x"])
    A_log = np.asarray(inputs["L0_attn_A_log"])
    Dv = np.asarray(inputs["L0_attn_D"])
    xpw = np.asarray(inputs["L0_attn_x_proj_w"])
    dtw = np.asarray(inputs["L0_attn_dt_proj_w"])
    dtbv = np.asarray(inputs["L0_attn_dt_proj_b"])
    Aw = np.asarray(inputs["L0_ffn_A_w"])
    Bw = np.asarray(inputs["L0_ffn_B_w"])
    Wq = np.asarray(inputs["L0_ffn_Wq_w"])
    keys = np.asarray(inputs["L0_ffn_keys"])

    ident16 = np.eye(128, dtype=np.float16)
    ident32 = np.eye(128, dtype=np.float32)
    selm = np.zeros((64, 4 * 128), dtype=np.float16)
    for mm in range(4):
        for p in range(128):
            selm[(mm * 128 + p) // 8, mm * 128 + p] = 1.0

    maps = []
    for cid in range(8):
        k, j = cid // 2, cid % 2
        sl = slice(j * DH, (j + 1) * DH)
        maps.append(dict(
            x_in=np.ascontiguousarray(xs[k][:, sl]),
            awt=np.ascontiguousarray(Aw[:, sl].T),
            bwt=np.ascontiguousarray(Bw[sl, :].T),
            wqt=np.ascontiguousarray(Wq[:, sl].T),
            keyst=np.ascontiguousarray(keys.T),
            xpt=np.ascontiguousarray(xpw[:, sl].T),
            dtwt=np.ascontiguousarray(dtw[sl, :].T),
            dtb=np.ascontiguousarray(dtbv[sl].reshape(NDT, 128).T),
            alog=np.ascontiguousarray(
                A_log[sl].reshape(NDT, 128, NST).transpose(1, 0, 2)
                .reshape(128, NDT * NST)),
            dvec=np.ascontiguousarray(Dv[sl].reshape(NDT, 128).T),
            ident16=ident16, ident32=ident32, sel=selm,
        ))
    return maps


def kernel(**inputs):
    seq_len = 4096
    nc = build_program(seq_len)
    maps = _shard_inputs(inputs)
    res = run_bass_kernel_spmd(nc, maps, core_ids=list(range(8)))
    full = np.empty((4, seq_len, DMODEL), dtype=np.float32)
    for cid in range(8):
        k, j = cid // 2, cid % 2
        full[k][:, j * DH:(j + 1) * DH] = res.results[cid]["out"]
    return full
